# version 06
# baseline (speedup 1.0000x reference)
import numpy as np
import concourse.bass as bass
import concourse.mybir as mybir
from concourse.bass_utils import run_bass_kernel_spmd
from contextlib import ExitStack

F32 = mybir.dt.float32
F32R = mybir.dt.float32r
AF = mybir.ActivationFunctionType
ALU = mybir.AluOpType
AX = mybir.AxisListType

D = 2048
L = 2048
NCORES = 8
NST = 16
TB = 512
NB = L // TB
NTT = TB // 128
NET = D // 128
NCH = D // 512
EPS = 1e-5
GSZ = 8


def build_nc():
    nc = bass.Bass("TRN2", target_bir_lowering=False)

    xh = nc.dram_tensor("x", [L, D], F32, kind="ExternalInput").ap()
    wdth = nc.dram_tensor("wdt_tiled", [NET, 128, NET, 128], F32, kind="ExternalInput").ap()
    woh = nc.dram_tensor("woT", [D, D], F32, kind="ExternalInput").ap()
    wfh = nc.dram_tensor("wfT", [D, D], F32, kind="ExternalInput").ap()
    wxh = nc.dram_tensor("wxT", [D, 2 * NST], F32, kind="ExternalInput").ap()
    alogh = nc.dram_tensor("alog", [D, NST], F32, kind="ExternalInput").ap()
    dtbh = nc.dram_tensor("dtb", [D], F32, kind="ExternalInput").ap()
    xbh = nc.dram_tensor("xb", [2 * NST], F32, kind="ExternalInput").ap()
    obh = nc.dram_tensor("ob", [1, D], F32, kind="ExternalInput").ap()
    fbh = nc.dram_tensor("fb", [1, D], F32, kind="ExternalInput").ap()
    onesch = nc.dram_tensor("ones_col", [128, 4], F32, kind="ExternalInput").ap()
    identh = nc.dram_tensor("ident", [128, 128], F32, kind="ExternalInput").ap()
    cvalh = nc.dram_tensor("cvals", [128, 3], F32, kind="ExternalInput").ap()
    bcscr = nc.dram_tensor("bc_scr", [2 * NST, TB], F32, kind="Internal").ap()
    alscr = nc.dram_tensor("al_scr", [4, 512], F32, kind="Internal").ap()
    outh = nc.dram_tensor("out", [L, D], F32, kind="ExternalOutput").ap()

    stack = ExitStack()
    sb = lambda name, shape, dt=F32: stack.enter_context(nc.sbuf_tensor(name, shape, dt))
    ps = lambda name: stack.enter_context(nc.psum_tensor(name, [128, 512], F32))

    AR1 = sb("ar1", [128, 8192])
    AR2 = sb("ar2", [128, 8192])
    BBA = sb("bba", [128, 8192])
    CBA = sb("cba", [128, 8192])
    PRD = sb("prd", [128, 4096])
    STG = sb("stg", [128, 4096])
    WDT = sb("wdts", [128, 4096])
    sWx = sb("swx", [128, 16 * 32])
    sA = sb("sa_", [128, 256])
    sState = sb("sstate", [128, 256])
    sDtb = sb("sdtb", [128, 16])
    sXb = sb("sxb", [32, 1])
    sOnesC = sb("sonesc", [128, 4])
    sIdent = sb("sident", [128, 128])
    sCV = sb("scv", [128, 3])
    sBC = sb("sbc", [32, 512])
    sDT = sb("sdt", [128, 1024])
    sDTX = sb("sdtx", [128, 1024])
    sDA = sb("sda", [128, 1024])
    sDBX = sb("sdbx", [128, 512])
    sH = sb("sh", [128, 512])
    sYr = sb("syr", [128, 512])
    sSum = sb("ssum", [128, 1])
    sSq = sb("ssq", [128, 1])
    sM = sb("sm", [128, 1])
    sVar = sb("svar", [128, 1])
    sT2 = sb("st2", [128, 1])
    sR = sb("sr", [128, 1])
    sMRn = sb("smrn", [128, 1])

    ar1_xnT = AR1[:].rearrange("p (e t) -> p e t", e=NET)
    ar1_x1 = AR1[:].rearrange("p (tt d) -> p tt d", tt=NTT)
    ar2_y = AR2[:].rearrange("p (e t) -> p e t", e=NET)
    ar2_xn2T = ar2_y
    bb_n = BBA[:].rearrange("p (n t) -> p n t", n=NST)
    bb_att = BBA[:].rearrange("p (tt d) -> p tt d", tt=NTT)
    cb_n = CBA[:].rearrange("p (n t) -> p n t", n=NST)
    cb_wo = [CBA[:, 0:512], CBA[:, 512:1024]]
    cb_wf = [CBA[:, 1024:1536], CBA[:, 1536:2048]]
    cb_ob = CBA[:, 2048:4096]
    cb_fb = CBA[:, 4096:6144]
    cb_al = CBA[:, 6144:8192]
    cb_al3 = CBA[:, 6144:8192].rearrange("p (c t) -> p c t", c=4)
    prd_g = PRD[:].rearrange("p (n t) -> p n t", n=GSZ)
    prd_red = PRD[:].rearrange("p (n t) -> p t n", n=GSZ)
    prd_xrl = [PRD[:, 0:2048], PRD[:, 2048:4096]]
    slotX = STG[:, 0:2048]
    slotN = STG[:, 2048:4096]
    wdt_t = WDT[:].rearrange("p (b k e) -> p b k e", b=2, k=NET)
    wdt_half = [WDT[:, 0:2048], WDT[:, 2048:4096]]
    swx_t = sWx[:].rearrange("p (k e) -> p k e", k=NET)
    sDTv = [sDT[:, 0:512], sDT[:, 512:1024]]
    sDTXv = [sDTX[:, 0:512], sDTX[:, 512:1024]]
    sDAv = [sDA[:, 0:512], sDA[:, 512:1024]]

    P = [ps(f"pb{i}") for i in range(8)]

    class Sem:
        def __init__(self, name):
            self.h = stack.enter_context(nc.semaphore(name))
            self.cnt = 0

    def S(name):
        return Sem(name)

    dX = S("dX"); dW1 = S("dW1"); dW2 = S("dW2"); dW3 = S("dW3")
    dM = S("dM"); dXR = S("dXR"); dBB = S("dBB"); dBCS = S("dBCS")
    dALS = S("dALS"); dOUT = S("dOUT")
    tTP = S("tTP"); tDT = S("tDT"); tBC = S("tBC")
    tMO = S("tMO"); tMF = S("tMF"); tAL = S("tAL")
    sEV = S("sEV"); sSP = S("sSP"); sDAc = S("sDAc"); sNRM = S("sNRM")
    sSTc = S("sSTc"); sBCE = S("sBCE"); sSGN = S("sSGN")
    vST = S("vST"); vDTXc = S("vDTXc"); vSCAN = S("vSCAN"); vY = S("vY")
    vEV = S("vEV"); vRES = S("vRES"); vBIN = S("vBIN"); vNRM = S("vNRM")
    barT = S("barT"); barV = S("barV"); barS = S("barS")

    sched = {"sync": [], "tensor": [], "vector": [], "scalar": [], "gpsimd": []}

    def op(eng, build, waits=(), incs=()):
        for s, v in waits:
            assert 0 <= v <= s.cnt, f"wait on future value: {eng} {v} > {s.cnt}"
        waits = tuple((s.h, v) for s, v in waits)
        incs2 = []
        for s, k in incs:
            s.cnt += k
            incs2.append((s.h, k))
        incs2 = tuple(incs2)

        def f(e):
            for h, v in waits:
                e.wait_ge(h, v)
            ins = build(e)
            for h, k in incs2:
                ins = ins.then_inc(h, k)
        sched[eng].append(f)

    def wait_only(eng, waits):
        for s, v in waits:
            assert 0 <= v <= s.cnt, f"wait on future value: {eng} {v} > {s.cnt}"
        waits = tuple((s.h, v) for s, v in waits)

        def f(e):
            for h, v in waits:
                e.wait_ge(h, v)
        sched[eng].append(f)

    op("sync", lambda e: e.dma_start(out=sA[:].rearrange("p (et n) -> p et n", n=NST),
                                     in_=alogh.rearrange("(et p) n -> p et n", p=128)), incs=[(dM, 16)])
    op("sync", lambda e: e.dma_start(out=sDtb[:], in_=dtbh.rearrange("(et p) -> p et", p=128),
                                     allow_slow_non_contiguous=True), incs=[(dM, 16)])
    op("sync", lambda e: e.dma_start(out=sXb[:], in_=xbh.rearrange("(e one) -> e one", one=1)), incs=[(dM, 16)])
    op("sync", lambda e: e.dma_start(out=sOnesC[:], in_=onesch), incs=[(dM, 16)])
    op("sync", lambda e: e.dma_start(out=sIdent[:], in_=identh), incs=[(dM, 16)])
    op("sync", lambda e: e.dma_start(out=sCV[:], in_=cvalh), incs=[(dM, 16)])
    op("sync", lambda e: e.dma_start(out=swx_t.bitcast(F32R), in_=wxh.rearrange("(k p) e -> p k e", p=128).bitcast(F32R)), incs=[(dM, 16)])
    dM_all = dM.cnt

    op("scalar", lambda e: e.activation(sA[:], sA[:], AF.Exp), waits=[(dM, dM_all)], incs=[(sEV, 1)])
    op("vector", lambda e: e.tensor_scalar_mul(sA[:], sA[:], -1.0), waits=[(sEV, sEV.cnt)], incs=[(vST, 1)])
    aready_v = vST.cnt

    for k in range(NET):
        kb = k % 2
        w = [(tAL, tAL.cnt - 1)] if k >= 2 else []
        op("gpsimd", lambda e, k=k, kb=kb: e.dma_start(out=wdt_half[kb].bitcast(F32R), in_=wfh[k * 128:(k + 1) * 128, :].bitcast(F32R)),
           waits=w, incs=[(dW3, 16)])
        w_abs = [(dW3, dW3.cnt)]
        if k >= 1:
            w_abs.append((tAL, tAL.cnt))
        op("scalar", lambda e, kb=kb: e.activation(slotN, wdt_half[kb], AF.Abs),
           waits=w_abs, incs=[(sEV, 1)])
        for c in range(4):
            op("tensor", lambda e, k=k, c=c: e.matmul(
                P[4 + c][0:4, :], sOnesC[:], slotN[:, c * 512:(c + 1) * 512],
                start=(k == 0), stop=(k == NET - 1)),
               waits=[(sEV, sEV.cnt)] if c == 0 else (),
               incs=[(tAL, 1)] if c == 3 else ())
    alpha_mm_done = tAL.cnt
    for c in range(4):
        w_ev = [(tAL, alpha_mm_done)]
        if c > 0:
            w_ev.append((dALS, dALS.cnt))
        op("vector", lambda e, c=c: e.tensor_scalar_mul(sYr[0:1, :], P[4 + c][0:1, :], 1.0 / D),
           waits=w_ev, incs=[(vST, 1)])
        op("sync", lambda e, c=c: e.dma_start(out=alscr[c:c + 1, :], in_=sYr[0:1, :]),
           waits=[(vST, vST.cnt)], incs=[(dALS, 16)])
    mk_als = dALS.cnt

    def ln_stats(xtile_ap, wait_v, wait_s):
        op("scalar", lambda e: e.activation(slotN, xtile_ap, AF.Copy, accum_out=sSum[:]),
           waits=list(wait_s) + [(vST, vST.cnt)], incs=[(sSTc, 1)])
        op("scalar", lambda e: e.activation(slotN, xtile_ap, AF.Square, accum_out=sSq[:]),
           incs=[(sSTc, 1)])
        mk_s = sSTc.cnt
        op("vector", lambda e: e.tensor_scalar_mul(sM[:], sSum[:], 1.0 / D), waits=[(sSTc, mk_s)])
        op("vector", lambda e: e.tensor_scalar_mul(sVar[:], sSq[:], 1.0 / D))
        op("vector", lambda e: e.tensor_scalar(sT2[:], sM[:], sM[:], None, ALU.mult))
        op("vector", lambda e: e.tensor_copy(sMRn[:], sM[:]))
        op("vector", lambda e: e.tensor_sub(sVar[:], sVar[:], sT2[:]), incs=[(vST, 1)])
        mk_v = vST.cnt
        op("scalar", lambda e: e.activation(sT2[:], sVar[:], AF.Ln, bias=sCV[:, 0:1]), waits=[(vST, mk_v)])
        op("scalar", lambda e: e.activation(sR[:], sT2[:], AF.Exp, scale=sCV[:, 1:2]), incs=[(sSTc, 1)])
        return sSTc.cnt, vST.cnt

    def xrow(b, tt):
        return b * TB + tt * 128

    mk_blk = {}

    for b in range(NB):
        if b > 0:
            op("tensor", lambda e: e.nop(), incs=[(barT, 1)])
            op("vector", lambda e: e.nop(), incs=[(barV, 1)])
            op("scalar", lambda e: e.nop(), incs=[(barS, 1)])
            wait_only("tensor", [(barV, b), (barS, b)])
            wait_only("vector", [(barT, b), (barS, b)])
            wait_only("scalar", [(barT, b), (barV, b)])

        for tt in range(NTT):
            w = [(vNRM, vNRM.cnt)] if (b > 0 or tt > 0) else []
            op("sync", lambda e, b=b, tt=tt: e.dma_start(out=slotX, in_=xh[xrow(b, tt):xrow(b, tt) + 128, :]),
               waits=w, incs=[(dX, 16)])
            mk_x = dX.cnt
            w_s1 = [(dX, mk_x)]
            if b == 0 and tt == 0:
                w_s1.append((tAL, alpha_mm_done))
            ln_stats(slotX, [(dX, mk_x)], w_s1)
            op("vector", lambda e: e.tensor_scalar(slotN, slotX, sM[:], sR[:], ALU.subtract, ALU.mult),
               waits=[(tTP, tTP.cnt), (sSTc, sSTc.cnt)], incs=[(vNRM, 1)])
            mk_n = vNRM.cnt
            for et in range(NET):
                pb = P[et % 2]
                w_t = [(vNRM, mk_n)]
                if b == 0 and tt == 0 and et == 0:
                    w_t.append((dM, dM_all))
                if tTP.cnt >= 2:
                    w_t.append((sEV, sEV.cnt - 1))
                op("tensor", lambda e, et=et, pb=pb: e.transpose(pb[:, 0:128], slotN[:, et * 128:(et + 1) * 128], sIdent[:]),
                   waits=w_t, incs=[(tTP, 1)])
                op("scalar", lambda e, et=et, tt=tt, pb=pb: e.copy(ar1_xnT[:, et, tt * 128:(tt + 1) * 128].bitcast(F32R), pb[:, 0:128]),
                   waits=[(tTP, tTP.cnt)], incs=[(sEV, 1)])
        mk_ar1 = sEV.cnt

        w_xp = [(dM, dM_all), (sEV, mk_ar1)]
        if b > 0:
            w_xp.append((sSP, sSP.cnt - 1))
        for k in range(NET):
            op("tensor", lambda e, k=k: e.matmul(
                P[2][0:32, :], swx_t[:, k, :].bitcast(F32R), ar1_xnT[:, k, :].bitcast(F32R),
                start=(k == 0), stop=(k == NET - 1)),
               waits=w_xp if k == 0 else (),
               incs=[(tBC, 1)] if k == NET - 1 else ())
        op("vector", lambda e: e.tensor_scalar(sBC[:], P[2][0:32, :], sXb[:], None, ALU.add),
           waits=[(tBC, tBC.cnt)], incs=[(sBCE, 1)])
        mk_bc = sBCE.cnt

        op("sync", lambda e: e.dma_start(out=bcscr, in_=sBC[:]), waits=[(sBCE, mk_bc)], incs=[(dBCS, 16)])
        mk_bcs = dBCS.cnt
        w_bb = [(dBCS, mk_bcs)]
        if b > 0:
            w_bb += [(vRES, mk_blk["vRES"]), (sSTc, mk_blk["sSTc"]), (tMF, mk_blk["tMF"]),
                     (vEV, mk_blk["vEV"]), (vBIN, mk_blk["vBIN"])]
        op("sync", lambda e: e.dma_start(out=bb_n, in_=bcscr[0:NST, :].partition_broadcast(128)),
           waits=w_bb, incs=[(dBB, 16)])
        op("sync", lambda e: e.dma_start(out=cb_n.bitcast(F32R), in_=bcscr[NST:2 * NST, :].partition_broadcast(128).bitcast(F32R)),
           incs=[(dBB, 16)])
        mk_bcb = dBB.cnt

        for et in range(NET):
            eb = et % 2
            if et < 2:
                w_d = [(tDT, tDT.cnt)]
                if b == 0:
                    w_d.append((tAL, alpha_mm_done))
            else:
                w_d = [(tDT, tDT.cnt - 1)]
            op("gpsimd", lambda e, et=et, eb=eb: e.dma_start(out=wdt_t[:, eb, :, :].bitcast(F32R), in_=wdth[et].bitcast(F32R)),
               waits=w_d, incs=[(dW1, 16)])
            mk_w = dW1.cnt
            pb = P[2 + eb]
            w_mm = [(dW1, mk_w)]
            if b == 0 and et < 2:
                w_mm.append((sEV, mk_ar1))
            if et == 0:
                w_mm.append((sBCE, mk_bc))
            elif b > 0 or et >= 2:
                w_mm.append((sSP, sSP.cnt - 1))
            for k in range(NET):
                op("tensor", lambda e, eb=eb, k=k, pb=pb: e.matmul(
                    pb[:], wdt_t[:, eb, k, :].bitcast(F32R), ar1_xnT[:, k, :].bitcast(F32R),
                    start=(k == 0), stop=(k == NET - 1)),
                   waits=w_mm if k == 0 else (),
                   incs=[(tDT, 1)] if k == NET - 1 else ())
            mk_dtmm = tDT.cnt
            w_sp = [(tDT, mk_dtmm)]
            if b > 0 or et >= 2:
                w_sp.append((vDTXc, vDTXc.cnt - 1))
            op("scalar", lambda e, et=et, pb=pb: e.activation(pb[:], pb[:], AF.Exp, bias=sDtb[:, et:et + 1]),
               waits=w_sp)
            op("scalar", lambda e, eb=eb, pb=pb: e.activation(sDTv[eb], pb[:], AF.Ln, bias=sCV[:, 2:3]), incs=[(sSP, 1)])
            mk_sp = sSP.cnt
            op("vector", lambda e, eb=eb, et=et: e.tensor_mul(sDTXv[eb], sDTv[eb], ar1_xnT[:, et, :]),
               waits=[(sSP, mk_sp)], incs=[(vDTXc, 1)])

            for n in range(NST):
                g = (b * NET + et) * NST + n
                nb2 = n % 2
                w_e = []
                if b == 0 and et == 0 and n == 0:
                    w_e.append((vST, aready_v))
                if g >= 2:
                    w_e.append((vSCAN, vSCAN.cnt - 1))
                op("scalar", lambda e, et=et, n=n, nb2=nb2, eb=eb: e.activation(
                    sDAv[nb2], sDTv[eb], AF.Exp, scale=sA[:, et * 16 + n:et * 16 + n + 1]),
                   waits=w_e, incs=[(sDAc, 1)])
                mk_da = sDAc.cnt
                w_v = [(dBB, mk_bcb)] if (et == 0 and n == 0) else []
                op("vector", lambda e, et=et, n=n, eb=eb: e.tensor_mul(sDBX[:], sDTXv[eb], bb_n[:, n, :]),
                   waits=w_v)
                init = 0.0 if b == 0 else sState[:, et * 16 + n:et * 16 + n + 1]
                op("vector", lambda e, nb2=nb2, init=init: e.tensor_tensor_scan(
                    sH[:], sDAv[nb2], sDBX[:], init, ALU.mult, ALU.add),
                   waits=[(sDAc, mk_da)], incs=[(vSCAN, 1)])
                op("vector", lambda e, et=et, n=n: e.tensor_mul(prd_g[:, n % GSZ, :], sH[:], cb_n[:, n, :]))
                op("vector", lambda e, et=et, n=n: e.tensor_copy(sState[:, et * 16 + n:et * 16 + n + 1], sH[:, 511:512]))
                if n % GSZ == GSZ - 1:
                    if n // GSZ == 0:
                        for c in range(8):
                            op("vector", lambda e, c=c: e.tensor_reduce(
                                sYr[:, c * 64:(c + 1) * 64], prd_red[:, c * 64:(c + 1) * 64, :], AX.X, ALU.add))
                    else:
                        for c in range(8):
                            def _red(e, et=et, c=c):
                                with nc.allow_low_precision(reason="f32r out rounding; accumulation is fp32 internal"):
                                    return e.tensor_reduce(
                                        ar2_y[:, et, c * 64:(c + 1) * 64].bitcast(F32R),
                                        prd_red[:, c * 64:(c + 1) * 64, :], AX.X, ALU.add)
                            op("vector", _red)
                        op("vector", lambda e, et=et: e.tensor_add(ar2_y[:, et, :].bitcast(F32R), ar2_y[:, et, :], sYr[:]),
                           incs=[(vY, 1)])
        mk_y = vY.cnt
        mk_scan_all = vSCAN.cnt

        op("sync", lambda e: e.dma_start(out=cb_ob.bitcast(F32R), in_=obh.partition_broadcast(128).bitcast(F32R)),
           waits=[(vY, mk_y)], incs=[(dBB, 16)])
        op("sync", lambda e: e.dma_start(out=cb_fb.bitcast(F32R), in_=fbh.partition_broadcast(128).bitcast(F32R)),
           incs=[(dBB, 16)])
        op("sync", lambda e: e.dma_start(out=cb_al3.bitcast(F32R), in_=alscr.partition_broadcast(128).bitcast(F32R)),
           waits=[(dALS, mk_als)], incs=[(dBB, 16)])
        mk_allbc = dBB.cnt

        tMO_base = tMO.cnt
        for ch in range(NCH):
            for k in range(NET):
                kb = k % 2
                if ch == 0 and k < 2:
                    w_d = [(vY, mk_y)]
                else:
                    w_d = [(tMO, tMO.cnt - 1)]
                op("gpsimd", lambda e, ch=ch, k=k, kb=kb: e.dma_start(
                    out=cb_wo[kb].bitcast(F32R), in_=woh[k * 128:(k + 1) * 128, ch * 512:(ch + 1) * 512].bitcast(F32R)),
                   waits=w_d, incs=[(dW2, 16)])
                mk_wp = dW2.cnt
                for tt in range(NTT):
                    w_mm = []
                    if tt == 0:
                        w_mm.append((dW2, mk_wp))
                    if k == 0 and tt == 0:
                        w_mm.append((vY, mk_y))
                        if ch > 0:
                            w_mm.append((vEV, vEV.cnt))
                    op("tensor", lambda e, ch=ch, k=k, kb=kb, tt=tt: e.matmul(
                        P[4 + tt][:], ar2_y[:, k, tt * 128:(tt + 1) * 128].bitcast(F32R),
                        cb_wo[kb].bitcast(F32R),
                        start=(k == 0), stop=(k == NET - 1)),
                       waits=w_mm, incs=[(tMO, 1)] if tt == NTT - 1 else ())
            mk_ch = tMO_base + (ch + 1) * NET
            for tt in range(NTT):
                w_ev = [(tMO, mk_ch)]
                if ch == 0 and tt == 0:
                    w_ev.append((dBB, mk_allbc))
                op("vector", lambda e, ch=ch, tt=tt: e.tensor_add(
                    bb_att[:, tt, ch * 512:(ch + 1) * 512], P[4 + tt][:], cb_ob[:, ch * 512:(ch + 1) * 512]),
                   waits=w_ev, incs=[(vEV, 1)])
        mk_att = vEV.cnt
        mk_tMO_all = tMO.cnt

        for tt in range(NTT):
            att_t = bb_att[:, tt, :]
            r_mk, mr_mk = ln_stats(att_t, [(vEV, mk_att)], [(vEV, mk_att)])
            w_xr = [(vY, mk_y)] if tt < 2 else [(vRES, vRES.cnt)]
            op("sync", lambda e, b=b, tt=tt: e.dma_start(out=prd_xrl[tt % 2], in_=xh[xrow(b, tt):xrow(b, tt) + 128, :]),
               waits=w_xr, incs=[(dXR, 16)])
            mk_xr = dXR.cnt
            op("vector", lambda e, att_t=att_t: e.tensor_scalar(att_t, att_t, sM[:], sR[:], ALU.subtract, ALU.mult),
               waits=[(sSTc, r_mk)])
            op("vector", lambda e, tt=tt, att_t=att_t: e.tensor_add(
                ar1_x1[:, tt, :].bitcast(F32R), att_t, prd_xrl[tt % 2]),
               waits=[(dXR, mk_xr)], incs=[(vRES, 1)])
            mk_x1 = vRES.cnt
            r3_mk, mr3_mk = ln_stats(ar1_x1[:, tt, :], [(vRES, mk_x1)], [(vRES, mk_x1)])
            op("vector", lambda e, tt=tt: e.tensor_scalar(slotN, ar1_x1[:, tt, :], sM[:], sR[:], ALU.subtract, ALU.mult),
               waits=[(tTP, tTP.cnt), (sSTc, r3_mk)], incs=[(vNRM, 1)])
            mk_n3 = vNRM.cnt
            for et in range(NET):
                pb = P[et % 2]
                w_t = [(vNRM, mk_n3), (sEV, sEV.cnt - 1)]
                op("tensor", lambda e, et=et, pb=pb: e.transpose(pb[:, 0:128], slotN[:, et * 128:(et + 1) * 128], sIdent[:]),
                   waits=w_t, incs=[(tTP, 1)])
                w_e2 = [(tTP, tTP.cnt)]
                if tt == 0 and et == 0:
                    w_e2.append((tMO, mk_tMO_all))
                op("scalar", lambda e, et=et, tt=tt, pb=pb: e.copy(ar2_xn2T[:, et, tt * 128:(tt + 1) * 128].bitcast(F32R), pb[:, 0:128]),
                   waits=w_e2, incs=[(sEV, 1)])
        mk_xn2 = sEV.cnt

        tMF_base = tMF.cnt
        for ch in range(NCH):
            for k in range(NET):
                kb = k % 2
                if ch == 0 and k < 2:
                    w_d = [(vY, mk_y)]
                else:
                    w_d = [(tMF, tMF.cnt - 1)]
                op("gpsimd", lambda e, ch=ch, k=k, kb=kb: e.dma_start(
                    out=cb_wf[kb].bitcast(F32R), in_=wfh[k * 128:(k + 1) * 128, ch * 512:(ch + 1) * 512].bitcast(F32R)),
                   waits=w_d, incs=[(dW3, 16)])
                mk_wp = dW3.cnt
                op("scalar", lambda e, kb=kb: e.activation(cb_wf[kb].bitcast(F32R), cb_wf[kb], AF.Sign),
                   waits=[(dW3, mk_wp)], incs=[(sSGN, 1)])
                w_bn = [(sSGN, sSGN.cnt)]
                if ch == 0 and k == 0:
                    w_bn.append((dBB, mk_allbc))
                op("vector", lambda e, kb=kb, ch=ch: e.tensor_mul(cb_wf[kb].bitcast(F32R), cb_wf[kb], cb_al[:, ch * 512:(ch + 1) * 512]),
                   waits=w_bn, incs=[(vBIN, 1)])
                mk_bin = vBIN.cnt
                for tt in range(NTT):
                    w_mm = []
                    if tt == 0:
                        w_mm.append((vBIN, mk_bin))
                    if k == 0 and tt == 0:
                        w_mm.append((sEV, mk_xn2))
                        if ch > 0:
                            w_mm.append((vEV, vEV.cnt))
                    op("tensor", lambda e, ch=ch, k=k, kb=kb, tt=tt: e.matmul(
                        P[4 + tt][:], ar2_xn2T[:, k, tt * 128:(tt + 1) * 128].bitcast(F32R),
                        cb_wf[kb].bitcast(F32R),
                        start=(k == 0), stop=(k == NET - 1)),
                       waits=w_mm, incs=[(tMF, 1)] if tt == NTT - 1 else ())
            mk_ch = tMF_base + (ch + 1) * NET
            for tt in range(NTT):
                op("vector", lambda e, ch=ch, tt=tt: e.tensor_add(
                    bb_att[:, tt, ch * 512:(ch + 1) * 512], P[4 + tt][:], cb_fb[:, ch * 512:(ch + 1) * 512]),
                   waits=[(tMF, mk_ch)], incs=[(vEV, 1)])
        mk_f = vEV.cnt

        for tt in range(NTT):
            f_t = bb_att[:, tt, :]
            r_mk, mr_mk = ln_stats(f_t, [(vEV, mk_f)], [(vEV, mk_f)])
            op("vector", lambda e, f_t=f_t: e.tensor_scalar(f_t, f_t, sM[:], sR[:], ALU.subtract, ALU.mult),
               waits=[(sSTc, r_mk)])
            op("vector", lambda e, tt=tt, f_t=f_t: e.tensor_add(
                ar1_x1[:, tt, :].bitcast(F32R), f_t, ar1_x1[:, tt, :]),
               incs=[(vRES, 1)])
            op("sync", lambda e, b=b, tt=tt: e.dma_start(out=outh[xrow(b, tt):xrow(b, tt) + 128, :], in_=ar1_x1[:, tt, :]),
               waits=[(vRES, vRES.cnt)], incs=[(dOUT, 16)])

        mk_blk = {"vRES": vRES.cnt, "sSTc": sSTc.cnt, "tMF": tMF.cnt,
                  "vEV": vEV.cnt, "vBIN": vBIN.cnt}

    wait_only("sync", [(dOUT, dOUT.cnt)])

    block = stack.enter_context(nc.Block())
    with stack:
        @block.sync
        def _(eng):
            for f in sched["sync"]:
                f(eng)

        @block.tensor
        def _(eng):
            for f in sched["tensor"]:
                f(eng)

        @block.vector
        def _(eng):
            for f in sched["vector"]:
                f(eng)

        @block.scalar
        def _(eng):
            for f in sched["scalar"]:
                f(eng)

        @block.gpsimd
        def _(eng):
            for f in sched["gpsimd"]:
                f(eng)
    return nc


_NC_CACHE = None


def kernel(L0_attn_x, L0_attn_A_log, L0_attn_x_proj_w, L0_attn_x_proj_b,
           L0_attn_dt_proj_w, L0_attn_dt_proj_b, L0_attn_out_proj_w,
           L0_attn_out_proj_b, L0_ffn_weight, L0_ffn_bias):
    global _NC_CACHE
    x = np.ascontiguousarray(np.asarray(L0_attn_x, dtype=np.float32))
    wdtT = np.ascontiguousarray(np.asarray(L0_attn_dt_proj_w, dtype=np.float32).T)
    wdt_tiled = np.ascontiguousarray(wdtT.reshape(16, 128, 16, 128).transpose(2, 1, 0, 3))
    woT = np.ascontiguousarray(np.asarray(L0_attn_out_proj_w, dtype=np.float32).T)
    wfT = np.ascontiguousarray(np.asarray(L0_ffn_weight, dtype=np.float32).T)
    wxT = np.ascontiguousarray(np.asarray(L0_attn_x_proj_w, dtype=np.float32).T)
    alog = np.ascontiguousarray(np.asarray(L0_attn_A_log, dtype=np.float32))
    dtb = np.ascontiguousarray(np.asarray(L0_attn_dt_proj_b, dtype=np.float32))
    xb = np.ascontiguousarray(np.asarray(L0_attn_x_proj_b, dtype=np.float32))
    ob = np.ascontiguousarray(np.asarray(L0_attn_out_proj_b, dtype=np.float32).reshape(1, D))
    fb = np.ascontiguousarray(np.asarray(L0_ffn_bias, dtype=np.float32).reshape(1, D))

    if _NC_CACHE is None:
        _NC_CACHE = build_nc()
    nc = _NC_CACHE

    shared = {
        "wdt_tiled": wdt_tiled, "woT": woT, "wfT": wfT, "wxT": wxT,
        "alog": alog, "dtb": dtb, "xb": xb, "ob": ob, "fb": fb,
        "ones_col": np.ones((128, 4), dtype=np.float32),
        "cvals": np.tile(np.array([[EPS, -0.5, 1.0]], dtype=np.float32), (128, 1)),
        "ident": np.eye(128, dtype=np.float32),
    }
    in_maps = []
    for c in range(NCORES):
        m = dict(shared)
        m["x"] = np.ascontiguousarray(x[c])
        in_maps.append(m)
    res = run_bass_kernel_spmd(nc, in_maps, core_ids=list(range(NCORES)))
    out = np.stack([res.results[c]["out"] for c in range(NCORES)], axis=0)
    return out
